# version 07
# baseline (speedup 1.0000x reference)
import math
from contextlib import ExitStack

import numpy as np
import concourse.bass as bass
import concourse.mybir as mybir
from concourse.bass_utils import run_bass_kernel_spmd

F32 = mybir.dt.float32
BF16 = mybir.dt.bfloat16
AF = mybir.ActivationFunctionType
ALU = mybir.AluOpType
AX = mybir.AxisListType

P = 128
D = 2048
NJ = 16
TOWN = 2048
NB = 16
NKB = 32
NG = 4
NH = 16
KV = 512
SCALE = 1.0 / math.sqrt(128.0)
EXP_SHIFT = -12.0
EPS = 1e-5

SEMS = ("pe", "dv", "ac", "hw", "sw", "cc",
        "hcst", "hx0", "hx1", "hr", "hst", "hkt", "hvg", "hq0", "hq1",
        "hy0", "hy1", "sw12", "sws0", "sws1", "swr", "sw30", "sw31",
        "swv", "swq0", "swq1", "swo", "cck", "ccv", "sws2", "sws3")


class Ctx:
    def __init__(self, nc, stack):
        self.nc = nc
        self._stack = stack
        self.t, self.v, self.s, self.g, self.sy = (nc.tensor, nc.vector, nc.scalar,
                                                   nc.gpsimd, nc.sync)
        self.sem = {n: stack.enter_context(nc.semaphore(n)) for n in SEMS}
        self.ctr = {n: 0 for n in SEMS}

    def new_sem(self, name):
        self.sem[name] = self._stack.enter_context(self.nc.semaphore(name))
        self.ctr[name] = 0
        return name

    PARANOID = False
    _ENG = {"pe": "t", "dv": "v", "ac": "s", "hw": "sy", "sw": "g", "cc": "g"}

    def inc(self, instr, sem, k=1):
        instr.then_inc(self.sem[sem], k)
        self.ctr[sem] += k
        if Ctx.PARANOID and sem in ("pe", "dv", "ac"):
            getattr(self, self._ENG[sem]).wait_ge(self.sem[sem], self.ctr[sem])
        return self.ctr[sem]

    def wait(self, eng, sem, val):
        if val <= 0:
            return
        assert val <= self.ctr[sem], f"wait on future {sem}:{val}>{self.ctr[sem]}"
        eng.wait_ge(self.sem[sem], val)

    def dma(self, eng, out, in_, sem):
        d = eng.dma_start(out=out, in_=in_)
        return self.inc(d, sem, 16)

    def barrier_all(self):
        snap = dict(self.ctr)
        for eng in (self.t, self.v, self.s, self.g, self.sy):
            for n, v_ in snap.items():
                if v_ > 0:
                    eng.wait_ge(self.sem[n], v_)


class Slot:
    __slots__ = ("ap", "name", "free", "load_mark")

    def __init__(self, ap, name=""):
        self.ap = ap
        self.name = name
        self.free = None
        self.load_mark = 0

    def wait_free(self, cx, eng):
        if self.free is not None:
            cx.wait(eng, self.free[0], self.free[1])

    def set_free(self, sem, count):
        self.free = (sem, count)


def build(nc, stages=3, ffn_parts=4, dbg=False):
    stack = ExitStack()
    dt = nc.dram_tensor
    x_in = dt("x_in", [TOWN, D], F32, kind="ExternalInput").ap()
    qmk_in = dt("qmk", [P, P], F32, kind="ExternalInput").ap()
    dtab_in = dt("dtab", [P, 32], F32, kind="ExternalInput").ap()
    ident_in = dt("ident", [P, P], F32, kind="ExternalInput").ap()
    wq_b = dt("wq_b", [NH, NJ, P, P], F32, kind="ExternalInput").ap()
    wk_b = dt("wk_b", [NG, NJ, P, P], F32, kind="ExternalInput").ap()
    wv_in = dt("wv", [D, KV], F32, kind="ExternalInput").ap()
    wo_b = dt("wo_b", [NJ, NJ, P, P], F32, kind="ExternalInput").ap()
    w3t_b = dt("w3t_b", [NJ, NJ, P, P], F32, kind="ExternalInput").ap()
    w1t_in = dt("w1t", [8, P, P], F32, kind="ExternalInput").ap()
    w2t_in = dt("w2t", [8, P, P], F32, kind="ExternalInput").ap()
    b1t_in = dt("b1t", [P, 8], F32, kind="ExternalInput").ap()
    b2t_in = dt("b2t", [P, 8], F32, kind="ExternalInput").ap()
    bqt_in = dt("bqt", [P, NH], F32, kind="ExternalInput").ap()
    bkt_in = dt("bkt", [P, NG], F32, kind="ExternalInput").ap()
    bvr_in = dt("bvr", [1, KV], F32, kind="ExternalInput").ap()
    bot_in = dt("bot", [P, NJ], F32, kind="ExternalInput").ap()
    b3t_in = dt("b3t", [P, NJ], F32, kind="ExternalInput").ap()
    swb_in = dt("swb", [8, NJ, P, 1024], F32, kind="ExternalInput").ap()
    swbias_in = dt("swbias", [1, 8], F32, kind="ExternalInput").ap()
    y_out = dt("y", [TOWN, D], F32, kind="ExternalOutput").ap()
    dbg1 = dt("dbg1", [P, NJ * 1024], BF16, kind="ExternalOutput").ap()
    dbg2 = dt("dbg2", [P, NJ * 1024], BF16, kind="ExternalOutput").ap()
    dbg3 = dt("dbg3", [P, 96], F32, kind="ExternalOutput").ap()

    qsc = dt("qsc", [D, TOWN], BF16)
    kT_ci = dt("kT_ci", [KV, TOWN], BF16)
    kT_co = dt("kT_co", [2 * KV, TOWN], BF16)
    v_ci = dt("v_ci", [TOWN, KV], BF16)
    v_co = dt("v_co", [2 * TOWN, KV], BF16)
    r_ci = [dt(f"r_ci{i}", [1, 8], F32) for i in range(2)]
    r_co = [dt(f"r_co{i}", [1, 8], F32) for i in range(2)]
    RG = [[0, 1], [2, 3], [4, 5], [6, 7]]

    cx = Ctx(nc, stack)
    t, v, s, g, sy = cx.t, cx.v, cx.s, cx.g, cx.sy

    def sb(st_, name, shape, dtype):
        return st_.enter_context(nc.sbuf_tensor(name, shape, dtype))

    XT = sb(stack, "XT", [P, NJ * TOWN], F32)
    AB = sb(stack, "AB", [P, NJ * 1024], BF16)
    ident = sb(stack, "c_ident", [P, P], F32)
    qmk = sb(stack, "c_qmk", [P, P], F32)
    dtab = sb(stack, "c_dtab", [P, 32], F32)
    ones_bf = sb(stack, "ones_bf", [P, 1], BF16)
    ones_all = sb(stack, "ones_all", [P, P], BF16)
    ones_f = sb(stack, "ones_f", [P, 1], F32)
    onesr_f = sb(stack, "onesr_f", [1, P], F32)
    onesr_bf = sb(stack, "onesr_bf", [1, P], BF16)
    invD = sb(stack, "invD", [P, P], BF16)
    cshift = sb(stack, "cshift", [P, 1], F32)
    ceps = sb(stack, "ceps", [P, 1], F32)
    bqt = sb(stack, "c_bqt", [P, NH], F32)
    bkt = sb(stack, "c_bkt", [P, NG], F32)
    bot = sb(stack, "c_bot", [P, NJ], F32)
    b3h = sb(stack, "b3h", [P, NJ], F32)
    b1t = sb(stack, "c_b1t", [P, 8], F32)
    b2t = sb(stack, "c_b2t", [P, 8], F32)
    swbias = sb(stack, "c_swbias", [1, 8], F32)
    r_row = sb(stack, "r_row", [1, 96], F32)

    pb = [stack.enter_context(nc.psum_tensor(f"pb{i}", [P, 512], F32)) for i in range(8)]
    bank = [Slot(pb[i], f"pb{i}") for i in range(8)]

    for dst, src in [(ident, ident_in), (qmk, qmk_in), (dtab, dtab_in),
                     (bqt, bqt_in), (bkt, bkt_in), (bot, bot_in), (b1t, b1t_in),
                     (b2t, b2t_in), (swbias, swbias_in), (b3h, b3t_in)]:
        cx.dma(sy, dst[:, :], src, "hcst")
    hw0 = cx.ctr["hcst"]

    v.memset(ones_bf[:, :], 1.0)
    v.memset(ones_all[:, :], 1.0)
    v.memset(ones_f[:, :], 1.0)
    v.memset(onesr_f[:, :], 1.0)
    v.memset(onesr_bf[:, :], 1.0)
    v.memset(invD[:, :], 1.0 / 2048.0)
    v.memset(cshift[:, :], EXP_SHIFT)
    cx.inc(v.memset(ceps[:, :], EPS), "dv")
    cx.wait(v, "hcst", hw0)
    cx.inc(v.tensor_scalar(b3h[:, :], b3h[:, :], 0.5, None, ALU.mult), "dv")

    with ExitStack() as iost:
        xin = [Slot(sb(iost, f"xin{i}", [P, D], F32), f"xin{i}") for i in range(2)]
        for tb in range(NB):
            sl = xin[tb % 2]
            sl.wait_free(cx, sy)
            mark = cx.dma(sy, sl.ap[:, :], x_in[tb * P:(tb + 1) * P, :], f"hx{tb % 2}")
            last_dv = 0
            for jg in range(4):
                b = bank[jg % 2]
                b.wait_free(cx, t)
                cx.wait(t, f"hx{tb % 2}", mark)
                for jj in range(4):
                    j = jg * 4 + jj
                    mm = t.transpose(b.ap[:, jj * P:(jj + 1) * P],
                                     sl.ap[:, j * P:(j + 1) * P], ident[:, :])
                pm = cx.inc(mm, "pe")
                cx.wait(v, "pe", pm)
                dst = XT[:, :].rearrange("p (j tt) -> p j tt", tt=TOWN)[
                    :, jg * 4:(jg + 1) * 4, tb * P:(tb + 1) * P]
                e = v.tensor_copy(dst, b.ap[:, :].rearrange("p (j tt) -> p j tt", tt=P))
                last_dv = cx.inc(e, "dv")
                b.set_free("dv", last_dv)
            sl.set_free("dv", last_dv)

    def make_ln_scratch(st_, tag):
        return {
            "xb": sb(st_, f"xb_{tag}", [P, 512], BF16),
            "sq": sb(st_, f"sq_{tag}", [P, 512], BF16),
            "tmp": sb(st_, f"tmp_{tag}", [P, 512], F32),
            "MREP": sb(st_, f"MREP_{tag}", [P, 512], F32),
            "RREP": sb(st_, f"RREP_{tag}", [P, 512], F32),
        }

    def ln_chunk(L, cc):
        s1, s2 = bank[4], bank[5]
        s1.wait_free(cx, t)
        s2.wait_free(cx, t)
        pm = 0
        for j in range(NJ):
            xsl = XT[:, j * TOWN + cc * 512: j * TOWN + (cc + 1) * 512]
            dm = cx.inc(v.tensor_copy(L["xb"][:, :], xsl), "dv")
            cx.wait(s, "dv", dm)
            am = cx.inc(s.activation(L["sq"][:, :], L["xb"][:, :], AF.Square), "ac")
            cx.wait(t, "dv", dm)
            cx.inc(t.matmul(s1.ap[:, :], invD[:, :], L["xb"][:, :],
                            start=(j == 0), stop=(j == NJ - 1)), "pe")
            cx.wait(t, "ac", am)
            pm = cx.inc(t.matmul(s2.ap[:, :], invD[:, :], L["sq"][:, :],
                                 start=(j == 0), stop=(j == NJ - 1)), "pe")
            if j < NJ - 1:
                cx.wait(v, "pe", pm)
                cx.wait(s, "pe", pm)
        cx.wait(v, "pe", pm)
        mcp = cx.inc(v.tensor_copy(L["MREP"][:, :], s1.ap[:, :]), "dv")
        cx.wait(s, "dv", mcp)
        am = cx.inc(s.activation(L["tmp"][:, :], s1.ap[:, :], AF.Square), "ac")
        s1.set_free("ac", am)
        cx.wait(v, "ac", am)
        dm = cx.inc(v.scalar_tensor_tensor(L["tmp"][:, :], L["tmp"][:, :], -1.0,
                                           s2.ap[:, :], ALU.mult, ALU.add), "dv")
        s2.set_free("dv", dm)
        cx.wait(s, "dv", dm)
        lm = cx.inc(s.activation(L["tmp"][:, :], L["tmp"][:, :], AF.Ln,
                                 bias=ceps[:, 0:1]), "ac")
        cx.wait(s, "ac", lm)
        am = cx.inc(s.activation(L["RREP"][:, :], L["tmp"][:, :], AF.Exp,
                                 scale=-0.5), "ac")
        cx.wait(v, "ac", am)

    def ln_apply(L, j, cc, dst_ap):
        v.tensor_tensor(L["tmp"][:, :],
                        XT[:, j * TOWN + cc * 512: j * TOWN + (cc + 1) * 512],
                        L["MREP"][:, :], ALU.subtract)
        return cx.inc(v.tensor_tensor(dst_ap, L["tmp"][:, :], L["RREP"][:, :],
                                      ALU.mult), "dv")

    def ffn(call):
        fst = ExitStack()
        L = make_ln_scratch(fst, f"f{call}")
        w1t_sb = sb(fst, f"w1t_sb_{call}", [P, 8 * P], BF16)
        w2t_sb = sb(fst, f"w2t_sb_{call}", [P, 8 * P], BF16)
        selrep = sb(fst, f"selrep_{call}", [P, 32], F32)
        w1sel = [sb(fst, f"w1sel{k}_{call}", [P, P], BF16) for k in range(2)]
        w2gsel = [sb(fst, f"w2gsel{k}_{call}", [P, P], BF16) for k in range(2)]
        b1sel = sb(fst, f"b1sel_{call}", [P, 2], F32)
        b2comb = sb(fst, f"b2comb_{call}", [P, 1], F32)
        tmp8 = sb(fst, f"tmp8_{call}", [P, 8], F32)
        cx.dma(g, w1t_sb[:, :].rearrange("p (e b) -> p e b", b=P),
               w1t_in.rearrange("e a b -> a e b"), "sw12")
        sw0 = cx.dma(g, w2t_sb[:, :].rearrange("p (e b) -> p e b", b=P),
                     w2t_in.rearrange("e a b -> a e b"), "sw12")
        swt = [Slot(sb(fst, f"swt{i}_{call}", [P, 1024], BF16), f"swt{i}") for i in range(4)]
        rj = [sb(fst, f"rj{i}_{call}", [P, 1024], BF16) for i in range(2)]
        ajunk = sb(fst, f"ajunk_{call}", [P, 1024], BF16)
        pacc = sb(fst, f"pacc_{call}", [P, 8], F32)
        p16 = sb(fst, f"p16_{call}", [P, NJ], F32)
        hsb = [[sb(fst, f"hsb{k}_{i}_{call}", [P, 512], BF16) for i in range(2)]
               for k in range(2)]
        hsb_free = [[None, None], [None, None]]
        w3sb = [Slot(sb(fst, f"w3sb{i}_{call}", [P, NJ * P], BF16), f"w3sb{i}")
                for i in range(2)]

        for ci in range(2):
            ln_chunk(L, ci)
            for j in range(NJ):
                ln_apply(L, j, ci, AB[:, j * 1024 + ci * 512: j * 1024 + (ci + 1) * 512])
        lnT_dv = cx.ctr["dv"]
        if dbg and call == 0:
            cx.wait(sy, "dv", lnT_dv)
            cx.dma(sy, dbg1, AB[:, :], "hw")
        if ffn_parts <= 1:
            fst.close()
            return

        rj_free = [0, 0]
        p16_rd = 0
        for e8 in range(8):
            for j in range(NJ):
                sidx = (e8 * NJ + j) % 4
                ssl = swt[sidx]
                ssl.wait_free(cx, g)
                dmark = cx.dma(g, ssl.ap[:, :], swb_in[e8, j, :, :], f"sws{sidx}")
                cx.wait(v, f"sws{sidx}", dmark)
                cx.wait(v, "dv", lnT_dv)
                cx.wait(v, "ac", rj_free[j % 2])
                dm2 = cx.inc(v.tensor_tensor(rj[j % 2][:, :], ssl.ap[:, :],
                                             AB[:, j * 1024: j * 1024 + 1024],
                                             ALU.mult), "dv")
                ssl.set_free("dv", dm2)
                cx.wait(s, "dv", dm2)
                if j == 0:
                    cx.wait(s, "dv", p16_rd)
                am = cx.inc(s.activation(ajunk[:, :], rj[j % 2][:, :], AF.Copy,
                                         accum_out=p16[:, j:j + 1]), "ac")
                rj_free[j % 2] = am
            cx.wait(v, "ac", rj_free[(NJ - 1) % 2])
            p16_rd = cx.inc(v.tensor_reduce(pacc[:, e8:e8 + 1], p16[:, :], AX.X,
                                            ALU.add), "dv")
        b = bank[2]
        b.wait_free(cx, t)
        cx.wait(t, "dv", cx.ctr["dv"])
        pm = cx.inc(t.matmul(b.ap[0:1, 0:8], ones_f[:, :], pacc[:, :],
                             start=True, stop=True), "pe")
        cx.wait(v, "pe", pm)
        dm = cx.inc(v.tensor_copy(r_row[0:1, 0:8], b.ap[0:1, 0:8]), "dv")
        b.set_free("dv", dm)
        cx.wait(sy, "dv", dm)
        hm = cx.dma(sy, r_ci[call].ap(), r_row[0:1, 0:8], "hr")
        cx.wait(g, "hr", hm)
        ccm = cx.inc(g.collective_compute("AllReduce", ALU.add, replica_groups=RG,
                                          ins=[r_ci[call].ap()], outs=[r_co[call].ap()]),
                     "cc")
        cx.wait(g, "cc", ccm)
        sm = cx.dma(g, r_row[0:1, 8:16], r_co[call].ap(), "swr")
        cx.wait(v, "swr", sm)
        LG = r_row[0:1, 8:16]
        def vsync(instr):
            mk = cx.inc(instr, "dv")
            cx.wait(v, "dv", mk)
            return mk
        vsync(v.tensor_tensor(LG, LG, swbias[0:1, :], ALU.add))
        M0, NM0 = r_row[0:1, 16:17], r_row[0:1, 17:18]
        vsync(v.reduce_max(M0, LG, axis=AX.X))
        dm = vsync(v.tensor_scalar(NM0, M0, -1.0, None, ALU.mult))
        EX = r_row[0:1, 24:32]
        cx.wait(s, "dv", dm)
        am = cx.inc(s.activation(EX, LG, AF.Exp, bias=NM0[0:1, 0:1]), "ac")
        SS, RC = r_row[0:1, 18:19], r_row[0:1, 19:20]
        cx.wait(v, "ac", am)
        vsync(v.reduce_sum(SS, EX, axis=AX.X))
        vsync(v.reciprocal(RC, SS))
        PRB = r_row[0:1, 32:40]
        vsync(v.tensor_single_scalar(PRB, EX, RC[0:1, 0:1], ALU.mult))
        G1, G2 = r_row[0:1, 20:21], r_row[0:1, 21:22]
        EQ1, EQ2, MS = r_row[0:1, 40:48], r_row[0:1, 48:56], r_row[0:1, 56:64]
        GV1, GV2 = r_row[0:1, 64:72], r_row[0:1, 72:80]
        vsync(v.reduce_max(G1, PRB, axis=AX.X))
        vsync(v.tensor_single_scalar(EQ1, PRB, G1[0:1, 0:1], ALU.is_equal))
        vsync(v.tensor_scalar(MS, EQ1, -4.0, None, ALU.mult))
        vsync(v.tensor_tensor(MS, MS, PRB, ALU.add))
        vsync(v.reduce_max(G2, MS, axis=AX.X))
        vsync(v.tensor_single_scalar(EQ2, MS, G2[0:1, 0:1], ALU.is_equal))
        vsync(v.tensor_single_scalar(GV1, EQ1, G1[0:1, 0:1], ALU.mult))
        dm = vsync(v.tensor_single_scalar(GV2, EQ2, G2[0:1, 0:1], ALU.mult))
        b = bank[3]
        b.wait_free(cx, t)
        cx.wait(t, "dv", dm)
        t.matmul(b.ap[:, 0:8], onesr_f[:, :], EQ1, start=True, stop=True)
        t.matmul(b.ap[:, 8:16], onesr_f[:, :], EQ2, start=True, stop=True)
        t.matmul(b.ap[:, 16:24], onesr_f[:, :], GV1, start=True, stop=True)
        pm = cx.inc(t.matmul(b.ap[:, 24:32], onesr_f[:, :], GV2, start=True, stop=True),
                    "pe")
        cx.wait(v, "pe", pm)
        dm = cx.inc(v.tensor_copy(selrep[:, :], b.ap[:, 0:32]), "dv")
        b.set_free("dv", dm)
        cx.wait(v, "sw12", sw0)
        vs_mk = [0]
        for k in range(2):
            for e8 in range(8):
                mcol = selrep[:, k * 8 + e8: k * 8 + e8 + 1]
                gcol = selrep[:, 16 + k * 8 + e8: 16 + k * 8 + e8 + 1]
                wsl = w1t_sb[:, e8 * P:(e8 + 1) * P]
                w2sl = w2t_sb[:, e8 * P:(e8 + 1) * P]
                if e8 == 0:
                    cx.inc(v.tensor_scalar(w1sel[k][:, :], wsl, mcol, None, ALU.mult), "dv")
                    cx.inc(v.tensor_scalar(w2gsel[k][:, :], w2sl, gcol, None, ALU.mult), "dv")
                else:
                    cx.inc(v.scalar_tensor_tensor(w1sel[k][:, :], wsl, mcol,
                                                  w1sel[k][:, :], ALU.mult, ALU.add), "dv")
                    cx.inc(v.scalar_tensor_tensor(w2gsel[k][:, :], w2sl, gcol,
                                                  w2gsel[k][:, :], ALU.mult, ALU.add), "dv")
            mk_ = cx.inc(v.tensor_tensor(tmp8[:, :], b1t[:, :], selrep[:, k * 8:(k + 1) * 8],
                                   ALU.mult), "dv")
            cx.wait(v, "dv", mk_)
            mk = cx.inc(v.tensor_reduce(b1sel[:, k:k + 1], tmp8[:, :], AX.X, ALU.add), "dv")
            cx.wait(v, "dv", mk)
        mk_ = cx.inc(v.tensor_tensor(tmp8[:, :], selrep[:, 16:24], selrep[:, 24:32], ALU.add), "dv")
        cx.wait(v, "dv", mk_)
        mk_ = cx.inc(v.tensor_tensor(tmp8[:, :], b2t[:, :], tmp8[:, :], ALU.mult), "dv")
        cx.wait(v, "dv", mk_)
        sel_dv = cx.inc(v.tensor_reduce(b2comb[:, :], tmp8[:, :], AX.X, ALU.add), "dv")
        cx.wait(v, "dv", sel_dv)
        sel_waited = [False]
        if ffn_parts <= 2:
            fst.close()
            return

        for half in range(2):
            if half == 1:
                cx.wait(v, "pe", cx.ctr["pe"])
                for ci in range(2):
                    ln_chunk(L, 2 + ci)
                    for j in range(NJ):
                        ln_apply(L, j, 2 + ci,
                                 AB[:, j * 1024 + ci * 512: j * 1024 + (ci + 1) * 512])
            ln_dv = cx.ctr["dv"]
            for ci in range(2):
                cc = half * 2 + ci
                for h in range(NH):
                    absl = AB[:, h * 1024 + ci * 512: h * 1024 + (ci + 1) * 512]
                    bo_ = bank[4 + h % 2]
                    am = 0
                    for k in range(2):
                        bh = bank[2 * k + h % 2]
                        bh.wait_free(cx, t)
                        cx.wait(t, "dv", max(ln_dv, sel_dv))
                        pm = cx.inc(t.matmul(bh.ap[:, :], w1sel[k][:, :], absl,
                                             start=True, stop=True), "pe")
                        cx.wait(s, "pe", pm)
                        if not sel_waited[0]:
                            cx.wait(s, "dv", sel_dv)
                            sel_waited[0] = True
                        if hsb_free[k][h % 2] is not None:
                            cx.wait(s, *hsb_free[k][h % 2])
                        am = cx.inc(s.activation(hsb[k][h % 2][:, :], bh.ap[:, :], AF.Gelu,
                                                 bias=b1sel[:, k:k + 1]), "ac")
                        bh.set_free("ac", am)
                    bo_.wait_free(cx, t)
                    cx.wait(t, "ac", am)
                    t.matmul(bo_.ap[:, :], w2gsel[0][:, :], hsb[0][h % 2][:, :],
                             start=True, stop=False)
                    pm = cx.inc(t.matmul(bo_.ap[:, :], w2gsel[1][:, :], hsb[1][h % 2][:, :],
                                         start=False, stop=True), "pe")
                    hsb_free[0][h % 2] = ("pe", pm)
                    hsb_free[1][h % 2] = ("pe", pm)
                    cx.wait(v, "pe", pm)
                    dm = cx.inc(v.scalar_tensor_tensor(absl, bo_.ap[:, :], b2comb[:, 0:1],
                                                       absl, ALU.add, ALU.add), "dv")
                    bo_.set_free("dv", dm)
            z_dv = cx.ctr["dv"]
            if dbg and call == 0 and half == 0:
                cx.wait(sy, "dv", z_dv)
                cx.dma(sy, dbg2, AB[:, :], "hw")
                cx.dma(sy, dbg3[0:1, :], r_row[0:1, :], "hw")
            if ffn_parts <= 3:
                continue
            for i in range(NJ):
                wsl = w3sb[i % 2]
                wsl.wait_free(cx, g)
                dmark = cx.dma(g, wsl.ap[:, :].rearrange("p (j b) -> p j b", b=P),
                               w3t_b[i, :, :, :].rearrange("j a b -> a j b"),
                               f"sw3{i % 2}")
                cx.wait(v, f"sw3{i % 2}", dmark)
                wdv = cx.inc(v.tensor_scalar(wsl.ap[:, :], wsl.ap[:, :], 0.5, None,
                                             ALU.mult), "dv")
                for ci in range(2):
                    bw = bank[6 + ci]
                    bw.wait_free(cx, t)
                    cx.wait(t, "dv", max(z_dv, wdv))
                    for j in range(NJ):
                        mm = t.matmul(bw.ap[:, :], wsl.ap[:, j * P:(j + 1) * P],
                                      AB[:, j * 1024 + ci * 512: j * 1024 + (ci + 1) * 512],
                                      start=(j == 0), stop=(j == NJ - 1))
                    pm = cx.inc(mm, "pe")
                    cc = half * 2 + ci
                    xsl = XT[:, i * TOWN + cc * 512: i * TOWN + (cc + 1) * 512]
                    cx.wait(v, "pe", pm)
                    dm = cx.inc(v.scalar_tensor_tensor(xsl, bw.ap[:, :], b3h[:, i:i + 1],
                                                       xsl, ALU.add, ALU.add), "dv")
                    bw.set_free("dv", dm)
                wsl.set_free("pe", pm)
        fst.close()

    def attention():
        qst = ExitStack()
        bvr = sb(qst, "c_bvr", [1, KV], BF16)
        cx.dma(g, bvr[:, :], bvr_in, "swv")
        L = make_ln_scratch(qst, "a")
        wv_sb = sb(qst, "wv_sb", [P, NJ * KV], BF16)
        wqk = [Slot(sb(qst, f"wqk{i}", [P, NJ * P], BF16), f"wqk{i}") for i in range(2)]
        ev16 = [Slot(sb(qst, "ev16_0", [P, 1024], BF16), "ev16_0")]
        wv_mark = cx.dma(g, wv_sb[:, :].rearrange("p (j k) -> p j k", k=KV),
                         wv_in.rearrange("(j p) k -> p j k", p=P), "swv")
        for half in range(2):
            hsl = slice(half * 1024, (half + 1) * 1024)
            if half == 1:
                cx.wait(v, "pe", cx.ctr["pe"])
            for ci in range(2):
                ln_chunk(L, half * 2 + ci)
                for j in range(NJ):
                    ln_apply(L, j, half * 2 + ci,
                             AB[:, j * 1024 + ci * 512: j * 1024 + (ci + 1) * 512])
            ln_dv = cx.ctr["dv"]
            for i in range(NH + NG):
                wsl = wqk[i % 2]
                wsl.wait_free(cx, g)
                src = wq_b[i, :, :, :] if i < NH else wk_b[i - NH, :, :, :]
                dmark = cx.dma(g, wsl.ap[:, :].rearrange("p (j b) -> p j b", b=P),
                              src.rearrange("j a b -> a j b"), f"swq{i % 2}")
                b0, b1_ = bank[2 * (i % 2)], bank[2 * (i % 2) + 1]
                b0.wait_free(cx, t)
                b1_.wait_free(cx, t)
                cx.wait(t, f"swq{i % 2}", dmark)
                cx.wait(t, "dv", ln_dv)
                for ci, bw in ((0, b0), (1, b1_)):
                    for j in range(NJ):
                        mm = t.matmul(bw.ap[:, :], wsl.ap[:, j * P:(j + 1) * P],
                                      AB[:, j * 1024 + ci * 512: j * 1024 + (ci + 1) * 512],
                                      start=(j == 0), stop=(j == NJ - 1))
                pm = cx.inc(mm, "pe")
                wsl.set_free("pe", pm)
                esl = ev16[0]
                esl.wait_free(cx, v)
                cx.wait(v, "pe", pm)
                bias_col = bqt[:, i:i + 1] if i < NH else bkt[:, i - NH:i - NH + 1]
                v.tensor_scalar(esl.ap[:, 0:512], b0.ap[:, :], bias_col, None, ALU.add)
                dm = cx.inc(v.tensor_scalar(esl.ap[:, 512:1024], b1_.ap[:, :], bias_col,
                                            None, ALU.add), "dv")
                b0.set_free("dv", dm)
                b1_.set_free("dv", dm)
                cx.wait(sy, "dv", dm)
                dst = (qsc.ap()[i * P:(i + 1) * P, hsl] if i < NH else
                       kT_ci.ap()[(i - NH) * P:(i - NH + 1) * P, hsl])
                esl.set_free("hst", cx.dma(sy, dst, esl.ap[:, :], "hst"))
            for tbh in range(8):
                tb = half * 8 + tbh
                bv_ = bank[4 + tbh % 2]
                bv_.wait_free(cx, t)
                cx.wait(t, "swv", wv_mark)
                cx.wait(t, "dv", ln_dv)
                t.matmul(bv_.ap[:, :], onesr_bf[:, :], bvr[0:1, :], start=True, stop=False)
                for j in range(NJ):
                    lh = AB[:, j * 1024 + tbh * P: j * 1024 + (tbh + 1) * P]
                    mm = t.matmul(bv_.ap[:, :], lh, wv_sb[:, j * KV:(j + 1) * KV],
                                  start=False, stop=(j == NJ - 1))
                pm = cx.inc(mm, "pe")
                esl = ev16[0]
                esl.wait_free(cx, v)
                cx.wait(v, "pe", pm)
                dm = cx.inc(v.tensor_copy(esl.ap[:, 0:512], bv_.ap[:, :]), "dv")
                bv_.set_free("dv", dm)
                cx.wait(sy, "dv", dm)
                esl.set_free("hst", cx.dma(sy, v_ci.ap()[tb * P:(tb + 1) * P, :],
                                           esl.ap[:, 0:512], "hst"))
        qst.close()
        cx.barrier_all()

        cx.wait(g, "hst", cx.ctr["hst"])
        ck = cx.inc(g.collective_compute("AllGather", ALU.bypass, replica_groups=RG,
                                         ins=[kT_ci.ap()], outs=[kT_co.ap()]), "cck")
        cv = cx.inc(g.collective_compute("AllGather", ALU.bypass, replica_groups=RG,
                                         ins=[v_ci.ap()], outs=[v_co.ap()]), "ccv")

        if dbg:
            cx.wait(sy, "cck", ck)
            cx.wait(sy, "ccv", cv)
            cx.dma(sy, dbg1.rearrange("p (a t) -> p a t", t=TOWN),
                   kT_co.ap().rearrange("(a p) t -> p a t", p=P), "hw")
            cx.dma(sy, dbg2.rearrange("p (a t) -> p a t", t=KV),
                   v_co.ap().rearrange("(a p) t -> p a t", p=P), "hw")
        fst = ExitStack()
        MSK = sb(fst, "MSK", [P, 8 * 512], BF16)
        for z in range(8):
            for m in range(4):
                mi = v.tensor_single_scalar(MSK[:, z * 512 + m * P: z * 512 + (m + 1) * P],
                                            qmk[:, :], dtab[:, z * 4 + m: z * 4 + m + 1],
                                            ALU.is_ge)
                cx.inc(mi, "dv")
        msk_dv = cx.ctr["dv"]
        KT = Slot(sb(fst, "KTg", [P, 2 * TOWN], BF16), "KTg")
        VG = Slot(sb(fst, "VGg", [P, NKB * P], BF16), "VGg")
        QT = [Slot(sb(fst, f"QT{hh}", [P, 512], BF16), f"QT{hh}") for hh in range(2)]
        PT = [[Slot(sb(fst, f"PT{hh}_{pr}", [P, 512], BF16), f"PT{hh}_{pr}")
               for pr in range(2)] for hh in range(2)]
        rrec = sb(fst, "rrec", [P, 512], F32)
        wosb = [Slot(sb(fst, "wosb0", [P, NJ * P], BF16), "wosb0")]

        def kcol(kb):
            return (kb % 2) * TOWN + (kb // 2) * P

        def vcol(kb):
            return ((kb % 2) * 16 + kb // 2) * P

        for half in range(2):
            for ci in range(2):
                cc = half * 2 + ci
                nkb = 8 * cc + 8
                nloc = nkb // 2
                for gi in range(NG):
                    KT.wait_free(cx, sy)
                    cx.wait(sy, "cck", ck)
                    cx.dma(sy, KT.ap[:, 0:nloc * P],
                           kT_co.ap()[gi * P:(gi + 1) * P, 0:nloc * P], "hkt")
                    kt_mark = cx.dma(sy, KT.ap[:, TOWN:TOWN + nloc * P],
                                     kT_co.ap()[KV + gi * P:KV + (gi + 1) * P, 0:nloc * P],
                                     "hkt")
                    VG.wait_free(cx, sy)
                    cx.wait(sy, "ccv", cv)
                    vg_mark = 0
                    for par2, vco0, vg0 in ((0, 0, 0), (1, TOWN, 16)):
                        for b0 in range(0, nloc, 4):
                            nb_ = min(4, nloc - b0)
                            vg_mark = cx.dma(
                                sy,
                                VG.ap[:, :].rearrange("p (k c) -> p k c", c=P)
                                [:, vg0 + b0:vg0 + b0 + nb_, :],
                                v_co.ap()[vco0 + b0 * P:vco0 + (b0 + nb_) * P,
                                          gi * P:(gi + 1) * P]
                                .rearrange("(k p) c -> p k c", p=P), "hvg")
                    for hp in range(2):
                        h0 = 4 * gi + 2 * hp
                        for hh in range(2):
                            QT[hh].wait_free(cx, sy)
                            QT[hh].load_mark = cx.dma(
                                sy, QT[hh].ap[:, :],
                                qsc.ap()[(h0 + hh) * P:(h0 + hh + 1) * P,
                                         cc * 512:(cc + 1) * 512], f"hq{hh}")
                        sbank = [[bank[0], bank[1]], [bank[2], bank[3]]]
                        cbank = [bank[4], bank[5]]
                        sumb = [bank[6], bank[7]]
                        for hh in range(2):
                            cbank[hh].wait_free(cx, t)
                            sumb[hh].wait_free(cx, t)
                        stage = []
                        def emit_scores(kb):
                            par = kb & 1
                            info = []
                            for hh in range(2):
                                bS = sbank[hh][par]
                                bS.wait_free(cx, t)
                                if kb < 2:
                                    cx.wait(t, "hkt", kt_mark)
                                    cx.wait(t, f"hq{hh}", QT[hh].load_mark)
                                pm = cx.inc(t.matmul(bS.ap[:, :],
                                                     KT.ap[:, kcol(kb):kcol(kb) + P],
                                                     QT[hh].ap[:, :],
                                                     start=True, stop=True), "pe")
                                info.append((bS, pm))
                            return info
                        def emit_exp_mask(kb, info):
                            par = kb & 1
                            zz = kb - 8 * cc
                            out = []
                            for hh in range(2):
                                bS, pm = info[hh]
                                cx.wait(s, "pe", pm)
                                am = cx.inc(s.activation(PT[hh][par].ap[:, :], bS.ap[:, :],
                                                         AF.Exp, bias=cshift[:, 0:1],
                                                         scale=SCALE), "ac")
                                bS.set_free("ac", am)
                                dvm = 0
                                if zz >= 0:
                                    cx.wait(v, "ac", am)
                                    cx.wait(v, "dv", msk_dv)
                                    dvm = cx.inc(
                                        v.tensor_tensor(PT[hh][par].ap[:, :],
                                                        PT[hh][par].ap[:, :],
                                                        MSK[:, zz * 512:(zz + 1) * 512],
                                                        ALU.mult), "dv")
                                out.append((am, dvm))
                            return out
                        def emit_consume(kb, marks):
                            par = kb & 1
                            pm = 0
                            for hh in range(2):
                                am, dvm = marks[hh]
                                if dvm:
                                    cx.wait(t, "dv", dvm)
                                else:
                                    cx.wait(t, "ac", am)
                                if kb == 0:
                                    cx.wait(t, "hvg", vg_mark)
                                t.matmul(sumb[hh].ap[:, :], ones_all[:, :],
                                         PT[hh][par].ap[:, :],
                                         start=(kb == 0), stop=(kb == nkb - 1))
                                pm = cx.inc(t.matmul(cbank[hh].ap[:, :],
                                                     VG.ap[:, vcol(kb):vcol(kb) + P],
                                                     PT[hh][par].ap[:, :],
                                                     start=(kb == 0), stop=(kb == nkb - 1)),
                                            "pe")
                                PT[hh][par].set_free("pe", pm)
                            return pm
                        prev = None
                        dump_unit = dbg and half == 0 and cc == 1 and gi == 3 and hp == 0
                        for kb in range(nkb):
                            sinfo = emit_scores(kb)
                            if prev is not None:
                                pm = emit_consume(prev[0], prev[1])
                            marks = emit_exp_mask(kb, sinfo)
                            if dump_unit:
                                for hx in range(2):
                                    amx, dvmx = marks[hx]
                                    cx.wait(sy, "ac", amx)
                                    if dvmx:
                                        cx.wait(sy, "dv", dvmx)
                                    dd = dbg1 if hx == 0 else dbg2
                                    hwm = cx.dma(sy, dd[:, kb * 512:(kb + 1) * 512],
                                                 PT[hx][kb & 1].ap[:, :], "hw")
                                cx.wait(s, "hw", hwm)
                            prev = (kb, marks)
                        pm = emit_consume(prev[0], prev[1])
                        QT[0].set_free("pe", pm)
                        QT[1].set_free("pe", pm)
                        last_pe = cx.ctr["pe"]
                        dm_prev = 0
                        for hh in range(2):
                            cx.wait(s, "pe", last_pe)
                            if dm_prev:
                                cx.wait(s, "dv", dm_prev)
                            lm = cx.inc(s.activation(rrec[:, :], sumb[hh].ap[:, :],
                                                     AF.Ln), "ac")
                            cx.wait(s, "ac", lm)
                            am = cx.inc(s.activation(rrec[:, :], rrec[:, :],
                                                     AF.Exp, scale=-1.0), "ac")
                            sumb[hh].set_free("ac", am)
                            cx.wait(v, "ac", am)
                            dst = AB[:, (h0 + hh) * 1024 + ci * 512:
                                     (h0 + hh) * 1024 + (ci + 1) * 512]
                            dm = cx.inc(v.tensor_tensor(dst, cbank[hh].ap[:, :],
                                                        rrec[:, :], ALU.mult), "dv")
                            cbank[hh].set_free("dv", dm)
                            dm_prev = dm
                            if dump_unit:
                                cx.wait(sy, "dv", dm)
                                hwm = cx.dma(sy, dbg2[:, 8192 + hh * 512: 8192 + (hh + 1) * 512],
                                             dst, "hw")
                                hwm = cx.dma(g, dbg2[:, 9216 + hh * 512: 9216 + (hh + 1) * 512],
                                             rrec[:, :], "sw")
                                cx.wait(v, "sw", hwm)
                    KT.set_free("pe", cx.ctr["pe"])
                    VG.set_free("pe", cx.ctr["pe"])
            ctx_dv = cx.ctr["dv"]
            if dbg and half == 0:
                cx.wait(sy, "dv", ctx_dv)
                cx.dma(sy, dbg1, AB[:, :], "hw")
            for i in range(NJ):
                wsl = wosb[0]
                wsl.wait_free(cx, g)
                dmark = cx.dma(g, wsl.ap[:, :].rearrange("p (j b) -> p j b", b=P),
                               wo_b[i, :, :, :].rearrange("j a b -> a j b"), "swo")
                for ci in range(2):
                    bw = bank[6 + ci]
                    bw.wait_free(cx, t)
                    cx.wait(t, "swo", dmark)
                    cx.wait(t, "dv", ctx_dv)
                    for j in range(NJ):
                        mm = t.matmul(bw.ap[:, :], wsl.ap[:, j * P:(j + 1) * P],
                                      AB[:, j * 1024 + ci * 512: j * 1024 + (ci + 1) * 512],
                                      start=(j == 0), stop=(j == NJ - 1))
                    pm = cx.inc(mm, "pe")
                    cc = half * 2 + ci
                    xsl = XT[:, i * TOWN + cc * 512: i * TOWN + (cc + 1) * 512]
                    cx.wait(v, "pe", pm)
                    dm = cx.inc(v.scalar_tensor_tensor(xsl, bw.ap[:, :], bot[:, i:i + 1],
                                                       xsl, ALU.add, ALU.add), "dv")
                    bw.set_free("dv", dm)
                wsl.set_free("pe", pm)
        fst.close()

    cx.barrier_all()
    if stages >= 1:
        ffn(0)
        cx.barrier_all()
    if stages >= 2:
        attention()
        cx.barrier_all()
    if stages >= 3:
        ffn(1)
        cx.barrier_all()

    with ExitStack() as iost:
        yout = [Slot(sb(iost, f"yout{i}", [P, D], F32), f"yout{i}") for i in range(2)]
        final_hw = 0
        for tb in range(NB):
            sl = yout[tb % 2]
            dm = 0
            for jg in range(4):
                b = bank[jg % 2]
                b.wait_free(cx, t)
                for jj in range(4):
                    j = jg * 4 + jj
                    src = XT[:, j * TOWN + tb * P: j * TOWN + (tb + 1) * P]
                    mm = t.transpose(b.ap[:, jj * P:(jj + 1) * P], src, ident[:, :])
                pm = cx.inc(mm, "pe")
                cx.wait(v, "pe", pm)
                sl.wait_free(cx, v)
                e = v.tensor_copy(sl.ap[:, jg * 512:(jg + 1) * 512], b.ap[:, :])
                dm = cx.inc(e, "dv")
                b.set_free("dv", dm)
            cx.wait(sy, "dv", dm)
            ysem = f"hy{tb % 2}"
            sl.set_free(ysem, cx.dma(sy, y_out[tb * P:(tb + 1) * P, :],
                                     sl.ap[:, :], ysem))
        for ysem in ("hy0", "hy1"):
            sy.wait_ge(cx.sem[ysem], cx.ctr[ysem])
            g.wait_ge(cx.sem[ysem], cx.ctr[ysem])

    stack.close()
    return nc


STAGES = 3
FFN_PARTS = 4
DBG = False
dbg = None


def kernel(L0_attn_hidden_states, L0_attn_Wq, L0_attn_bq, L0_attn_Wk, L0_attn_bk,
           L0_attn_Wv, L0_attn_bv, L0_attn_Wo, L0_attn_bo, L0_ffn_switch_w,
           L0_ffn_switch_b, L0_ffn_w1, L0_ffn_b1, L0_ffn_w2, L0_ffn_b2, L0_ffn_w3,
           L0_ffn_b3):
    x = np.ascontiguousarray(np.asarray(L0_attn_hidden_states, np.float32))
    B, S, _ = x.shape
    f32 = lambda a: np.ascontiguousarray(np.asarray(a, np.float32))

    Wq, bq = f32(L0_attn_Wq), f32(L0_attn_bq)
    Wk, bk = f32(L0_attn_Wk), f32(L0_attn_bk)
    Wv, bv = f32(L0_attn_Wv), f32(L0_attn_bv)
    Wo, bo = f32(L0_attn_Wo), f32(L0_attn_bo)
    sw, swb_ = f32(L0_ffn_switch_w), f32(L0_ffn_switch_b)
    w1, b1 = f32(L0_ffn_w1), f32(L0_ffn_b1)
    w2, b2 = f32(L0_ffn_w2), f32(L0_ffn_b2)
    w3, b3 = f32(L0_ffn_w3), f32(L0_ffn_b3)

    wq_b = np.ascontiguousarray(Wq.reshape(NJ, P, NH, P).transpose(2, 0, 1, 3))
    wk_b = np.ascontiguousarray(Wk.reshape(NJ, P, NG, P).transpose(2, 0, 1, 3))
    wo_b = np.ascontiguousarray(Wo.reshape(NJ, P, NJ, P).transpose(2, 0, 1, 3))
    w3t_b = np.ascontiguousarray(w3.reshape(NJ, P, NJ, P).transpose(0, 2, 3, 1))
    w1t = np.ascontiguousarray(w1.transpose(0, 2, 1))
    w2t = np.ascontiguousarray(w2.transpose(0, 2, 1))
    b1t = np.ascontiguousarray(b1.T)
    b2t = np.ascontiguousarray(b2.T)
    bqt = np.ascontiguousarray(bq.reshape(NH, P).T)
    bkt = np.ascontiguousarray(bk.reshape(NG, P).T)
    bot = np.ascontiguousarray(bo.reshape(NJ, P).T)
    b3t = np.ascontiguousarray(b3.reshape(NJ, P).T)
    bvr = bv.reshape(1, KV)
    swbias = swb_.reshape(1, 8)
    sw5 = sw.reshape(8, 16, P, NJ, P)
    ident = np.eye(P, dtype=np.float32)
    qmk = (np.arange(P, dtype=np.float32)[None, :] -
           np.arange(P, dtype=np.float32)[:, None]).astype(np.float32)

    nc = bass.Bass("TRN2", target_bir_lowering=False, detect_race_conditions=False)
    build(nc, STAGES, FFN_PARTS, DBG)

    in_maps = []
    for c in range(8):
        b_, p_ = c // 2, c % 2
        xs = np.ascontiguousarray(x[b_].reshape(NKB, P, D)[p_::2].reshape(TOWN, D))
        dtabv = np.zeros((P, 32), np.float32)
        for z in range(8):
            for m in range(4):
                dtabv[:, z * 4 + m] = 128.0 * (z - 2 * m - p_)
        swb_c = np.ascontiguousarray(
            sw5[:, p_::2].transpose(0, 3, 4, 1, 2).reshape(8, NJ, P, 1024))
        in_maps.append({
            "x_in": xs, "qmk": qmk, "dtab": dtabv, "ident": ident,
            "wq_b": wq_b, "wk_b": wk_b, "wv": Wv, "wo_b": wo_b, "w3t_b": w3t_b,
            "w1t": w1t, "w2t": w2t, "b1t": b1t, "b2t": b2t, "bqt": bqt, "bkt": bkt,
            "bvr": bvr, "bot": bot, "b3t": b3t, "swb": swb_c, "swbias": swbias,
        })
    res = run_bass_kernel_spmd(nc, in_maps, core_ids=list(range(8)))
    if DBG:
        global dbg
        dbg = [{k: np.asarray(res.results[c][k]) for k in ("dbg1", "dbg2", "dbg3")}
               for c in range(8)]
    out = np.empty((B, S, D), np.float32)
    for c in range(8):
        b_, p_ = c // 2, c % 2
        out[b_].reshape(NKB, P, D)[p_::2] = res.results[c]["y"].reshape(NB, P, D)
    return out


# version 08
# speedup vs baseline: 1.2543x; 1.2543x over previous
import math
from contextlib import ExitStack

import numpy as np
import concourse.bass as bass
import concourse.mybir as mybir
from concourse.bass_utils import run_bass_kernel_spmd

F32 = mybir.dt.float32
BF16 = mybir.dt.bfloat16
AF = mybir.ActivationFunctionType
ALU = mybir.AluOpType
AX = mybir.AxisListType

P = 128
D = 2048
NJ = 16
TOWN = 2048
NB = 16
NKB = 32
NG = 4
NH = 16
KV = 512
SCALE = 1.0 / math.sqrt(128.0)
EXP_SHIFT = -12.0
EPS = 1e-5

SEMS = ("pe", "dv", "ac", "hw", "sw", "cc",
        "hcst", "hx0", "hx1", "hr", "hst", "hkt", "hvg", "hq0", "hq1",
        "hy0", "hy1", "sw12", "sws0", "sws1", "swr", "sw30", "sw31",
        "swv", "swq0", "swq1", "swo0", "swo1", "cck", "ccv", "sws2", "sws3")


class Ctx:
    def __init__(self, nc, stack):
        self.nc = nc
        self._stack = stack
        self.t, self.v, self.s, self.g, self.sy = (nc.tensor, nc.vector, nc.scalar,
                                                   nc.gpsimd, nc.sync)
        self.sem = {n: stack.enter_context(nc.semaphore(n)) for n in SEMS}
        self.ctr = {n: 0 for n in SEMS}

    def new_sem(self, name):
        self.sem[name] = self._stack.enter_context(self.nc.semaphore(name))
        self.ctr[name] = 0
        return name

    PARANOID = False
    _ENG = {"pe": "t", "dv": "v", "ac": "s", "hw": "sy", "sw": "g", "cc": "g"}

    def inc(self, instr, sem, k=1):
        instr.then_inc(self.sem[sem], k)
        self.ctr[sem] += k
        if Ctx.PARANOID and sem in ("pe", "dv", "ac"):
            getattr(self, self._ENG[sem]).wait_ge(self.sem[sem], self.ctr[sem])
        return self.ctr[sem]

    def wait(self, eng, sem, val):
        if val <= 0:
            return
        assert val <= self.ctr[sem], f"wait on future {sem}:{val}>{self.ctr[sem]}"
        eng.wait_ge(self.sem[sem], val)

    def dma(self, eng, out, in_, sem):
        d = eng.dma_start(out=out, in_=in_)
        return self.inc(d, sem, 16)

    def barrier_all(self):
        snap = dict(self.ctr)
        for eng in (self.t, self.v, self.s, self.g, self.sy):
            for n, v_ in snap.items():
                if v_ > 0:
                    eng.wait_ge(self.sem[n], v_)


class Slot:
    __slots__ = ("ap", "name", "free", "load_mark")

    def __init__(self, ap, name=""):
        self.ap = ap
        self.name = name
        self.free = None
        self.load_mark = 0

    def wait_free(self, cx, eng):
        if self.free is not None:
            cx.wait(eng, self.free[0], self.free[1])

    def set_free(self, sem, count):
        self.free = (sem, count)


def build(nc, stages=3, ffn_parts=4, dbg=False):
    stack = ExitStack()
    dt = nc.dram_tensor
    x_in = dt("x_in", [TOWN, D], F32, kind="ExternalInput").ap()
    qmk_in = dt("qmk", [P, P], F32, kind="ExternalInput").ap()
    dtab_in = dt("dtab", [P, 32], F32, kind="ExternalInput").ap()
    ident_in = dt("ident", [P, P], F32, kind="ExternalInput").ap()
    wq_b = dt("wq_b", [NH, P, NJ, P], F32, kind="ExternalInput").ap()
    wk_b = dt("wk_b", [NG, P, NJ, P], F32, kind="ExternalInput").ap()
    wv_in = dt("wv", [P, NJ, KV], F32, kind="ExternalInput").ap()
    wo_b = dt("wo_b", [NJ, P, NJ, P], F32, kind="ExternalInput").ap()
    w3t_b = dt("w3t_b", [NJ, P, NJ, P], F32, kind="ExternalInput").ap()
    w1t_in = dt("w1t", [P, 8, P], F32, kind="ExternalInput").ap()
    w2t_in = dt("w2t", [P, 8, P], F32, kind="ExternalInput").ap()
    b1t_in = dt("b1t", [P, 8], F32, kind="ExternalInput").ap()
    b2t_in = dt("b2t", [P, 8], F32, kind="ExternalInput").ap()
    bqt_in = dt("bqt", [P, NH], F32, kind="ExternalInput").ap()
    bkt_in = dt("bkt", [P, NG], F32, kind="ExternalInput").ap()
    bvr_in = dt("bvr", [1, KV], F32, kind="ExternalInput").ap()
    bot_in = dt("bot", [P, NJ], F32, kind="ExternalInput").ap()
    b3t_in = dt("b3t", [P, NJ], F32, kind="ExternalInput").ap()
    swb_in = dt("swb", [8, NJ, P, 1024], F32, kind="ExternalInput").ap()
    swbias_in = dt("swbias", [1, 8], F32, kind="ExternalInput").ap()
    y_out = dt("y", [TOWN, D], F32, kind="ExternalOutput").ap()
    dbg1 = dt("dbg1", [P, NJ * 1024], BF16, kind="ExternalOutput").ap()
    dbg2 = dt("dbg2", [P, NJ * 1024], BF16, kind="ExternalOutput").ap()
    dbg3 = dt("dbg3", [P, 96], F32, kind="ExternalOutput").ap()

    qsc = dt("qsc", [D, TOWN], BF16)
    kT_ci = dt("kT_ci", [KV, TOWN], BF16)
    kT_co = dt("kT_co", [2 * KV, TOWN], BF16)
    v_ci = dt("v_ci", [TOWN, KV], BF16)
    v_co = dt("v_co", [2 * TOWN, KV], BF16)
    r_ci = [dt(f"r_ci{i}", [1, 8], F32) for i in range(2)]
    r_co = [dt(f"r_co{i}", [1, 8], F32) for i in range(2)]
    RG = [[0, 1], [2, 3], [4, 5], [6, 7]]

    cx = Ctx(nc, stack)
    t, v, s, g, sy = cx.t, cx.v, cx.s, cx.g, cx.sy

    def sb(st_, name, shape, dtype):
        return st_.enter_context(nc.sbuf_tensor(name, shape, dtype))

    XT = sb(stack, "XT", [P, NJ * TOWN], F32)
    AB = sb(stack, "AB", [P, NJ * 1024], BF16)
    ident = sb(stack, "c_ident", [P, P], F32)
    qmk = sb(stack, "c_qmk", [P, P], F32)
    dtab = sb(stack, "c_dtab", [P, 32], F32)
    ones_bf = sb(stack, "ones_bf", [P, 1], BF16)
    ones_all = sb(stack, "ones_all", [P, P], BF16)
    ones_f = sb(stack, "ones_f", [P, 1], F32)
    onesr_f = sb(stack, "onesr_f", [1, P], F32)
    onesr_bf = sb(stack, "onesr_bf", [1, P], BF16)
    invD = sb(stack, "invD", [P, P], BF16)
    cshift = sb(stack, "cshift", [P, 1], F32)
    ceps = sb(stack, "ceps", [P, 1], F32)
    bqt = sb(stack, "c_bqt", [P, NH], F32)
    bkt = sb(stack, "c_bkt", [P, NG], F32)
    bot = sb(stack, "c_bot", [P, NJ], F32)
    b3h = sb(stack, "b3h", [P, NJ], F32)
    b1t = sb(stack, "c_b1t", [P, 8], F32)
    b2t = sb(stack, "c_b2t", [P, 8], F32)
    swbias = sb(stack, "c_swbias", [1, 8], F32)
    r_row = sb(stack, "r_row", [1, 96], F32)

    pb = [stack.enter_context(nc.psum_tensor(f"pb{i}", [P, 512], F32)) for i in range(8)]
    bank = [Slot(pb[i], f"pb{i}") for i in range(8)]

    for dst, src in [(ident, ident_in), (qmk, qmk_in), (dtab, dtab_in),
                     (bqt, bqt_in), (bkt, bkt_in), (bot, bot_in), (b1t, b1t_in),
                     (b2t, b2t_in), (swbias, swbias_in), (b3h, b3t_in)]:
        cx.dma(sy, dst[:, :], src, "hcst")
    hw0 = cx.ctr["hcst"]

    v.memset(ones_bf[:, :], 1.0)
    v.memset(ones_all[:, :], 1.0)
    v.memset(ones_f[:, :], 1.0)
    v.memset(onesr_f[:, :], 1.0)
    v.memset(onesr_bf[:, :], 1.0)
    v.memset(invD[:, :], 1.0 / 2048.0)
    v.memset(cshift[:, :], EXP_SHIFT)
    cx.inc(v.memset(ceps[:, :], EPS), "dv")
    cx.wait(v, "hcst", hw0)
    cx.inc(v.tensor_scalar(b3h[:, :], b3h[:, :], 0.5, None, ALU.mult), "dv")

    with ExitStack() as iost:
        xin = [Slot(sb(iost, f"xin{i}", [P, D], F32), f"xin{i}") for i in range(2)]
        for tb in range(NB):
            sl = xin[tb % 2]
            sl.wait_free(cx, sy)
            mark = cx.dma(sy, sl.ap[:, :], x_in[tb * P:(tb + 1) * P, :], f"hx{tb % 2}")
            last_dv = 0
            for jg in range(4):
                b = bank[jg % 2]
                b.wait_free(cx, t)
                cx.wait(t, f"hx{tb % 2}", mark)
                for jj in range(4):
                    j = jg * 4 + jj
                    mm = t.transpose(b.ap[:, jj * P:(jj + 1) * P],
                                     sl.ap[:, j * P:(j + 1) * P], ident[:, :])
                pm = cx.inc(mm, "pe")
                cx.wait(v, "pe", pm)
                dst = XT[:, :].rearrange("p (j tt) -> p j tt", tt=TOWN)[
                    :, jg * 4:(jg + 1) * 4, tb * P:(tb + 1) * P]
                e = v.tensor_copy(dst, b.ap[:, :].rearrange("p (j tt) -> p j tt", tt=P))
                last_dv = cx.inc(e, "dv")
                b.set_free("dv", last_dv)
            sl.set_free("dv", last_dv)

    def make_ln_scratch(st_, tag):
        return {
            "xb": sb(st_, f"xb_{tag}", [P, 512], BF16),
            "sq": sb(st_, f"sq_{tag}", [P, 512], BF16),
            "tmp": sb(st_, f"tmp_{tag}", [P, 512], F32),
            "MREP": sb(st_, f"MREP_{tag}", [P, 512], F32),
            "RREP": sb(st_, f"RREP_{tag}", [P, 512], F32),
        }

    def ln_chunk(L, cc):
        s1, s2 = bank[4], bank[5]
        s1.wait_free(cx, t)
        s2.wait_free(cx, t)
        pm = 0
        for j in range(NJ):
            xsl = XT[:, j * TOWN + cc * 512: j * TOWN + (cc + 1) * 512]
            dm = cx.inc(v.tensor_copy(L["xb"][:, :], xsl), "dv")
            cx.wait(s, "dv", dm)
            am = cx.inc(s.activation(L["sq"][:, :], L["xb"][:, :], AF.Square), "ac")
            cx.wait(t, "dv", dm)
            cx.inc(t.matmul(s1.ap[:, :], invD[:, :], L["xb"][:, :],
                            start=(j == 0), stop=(j == NJ - 1)), "pe")
            cx.wait(t, "ac", am)
            pm = cx.inc(t.matmul(s2.ap[:, :], invD[:, :], L["sq"][:, :],
                                 start=(j == 0), stop=(j == NJ - 1)), "pe")
            if j < NJ - 1:
                cx.wait(v, "pe", pm)
                cx.wait(s, "pe", pm)
        cx.wait(v, "pe", pm)
        mcp = cx.inc(v.tensor_copy(L["MREP"][:, :], s1.ap[:, :]), "dv")
        cx.wait(s, "dv", mcp)
        am = cx.inc(s.activation(L["tmp"][:, :], s1.ap[:, :], AF.Square), "ac")
        s1.set_free("ac", am)
        cx.wait(v, "ac", am)
        dm = cx.inc(v.scalar_tensor_tensor(L["tmp"][:, :], L["tmp"][:, :], -1.0,
                                           s2.ap[:, :], ALU.mult, ALU.add), "dv")
        s2.set_free("dv", dm)
        cx.wait(s, "dv", dm)
        lm = cx.inc(s.activation(L["tmp"][:, :], L["tmp"][:, :], AF.Ln,
                                 bias=ceps[:, 0:1]), "ac")
        cx.wait(s, "ac", lm)
        am = cx.inc(s.activation(L["RREP"][:, :], L["tmp"][:, :], AF.Exp,
                                 scale=-0.5), "ac")
        cx.wait(v, "ac", am)

    def ln_apply(L, j, cc, dst_ap):
        v.tensor_tensor(L["tmp"][:, :],
                        XT[:, j * TOWN + cc * 512: j * TOWN + (cc + 1) * 512],
                        L["MREP"][:, :], ALU.subtract)
        return cx.inc(v.tensor_tensor(dst_ap, L["tmp"][:, :], L["RREP"][:, :],
                                      ALU.mult), "dv")

    def ffn(call):
        fst = ExitStack()
        L = make_ln_scratch(fst, f"f{call}")
        w1t_sb = sb(fst, f"w1t_sb_{call}", [P, 8 * P], BF16)
        w2t_sb = sb(fst, f"w2t_sb_{call}", [P, 8 * P], BF16)
        selrep = sb(fst, f"selrep_{call}", [P, 32], F32)
        w1sel = [sb(fst, f"w1sel{k}_{call}", [P, P], BF16) for k in range(2)]
        w2gsel = [sb(fst, f"w2gsel{k}_{call}", [P, P], BF16) for k in range(2)]
        b1sel = sb(fst, f"b1sel_{call}", [P, 2], F32)
        b2comb = sb(fst, f"b2comb_{call}", [P, 1], F32)
        tmp8 = sb(fst, f"tmp8_{call}", [P, 8], F32)
        cx.dma(g, w1t_sb[:, :].rearrange("p (e b) -> p e b", b=P), w1t_in, "sw12")
        sw0 = cx.dma(g, w2t_sb[:, :].rearrange("p (e b) -> p e b", b=P), w2t_in,
                     "sw12")
        swt = [Slot(sb(fst, f"swt{i}_{call}", [P, 1024], BF16), f"swt{i}") for i in range(4)]
        rj = [sb(fst, f"rj{i}_{call}", [P, 1024], BF16) for i in range(2)]
        ajunk = sb(fst, f"ajunk_{call}", [P, 1024], BF16)
        pacc = sb(fst, f"pacc_{call}", [P, 8], F32)
        p16 = sb(fst, f"p16_{call}", [P, NJ], F32)
        hsb = [[sb(fst, f"hsb{k}_{i}_{call}", [P, 512], BF16) for i in range(2)]
               for k in range(2)]
        hsb_free = [[None, None], [None, None]]
        w3sb = [Slot(sb(fst, f"w3sb{i}_{call}", [P, NJ * P], BF16), f"w3sb{i}")
                for i in range(2)]

        for ci in range(2):
            ln_chunk(L, ci)
            for j in range(NJ):
                ln_apply(L, j, ci, AB[:, j * 1024 + ci * 512: j * 1024 + (ci + 1) * 512])
        lnT_dv = cx.ctr["dv"]
        if dbg and call == 0:
            cx.wait(sy, "dv", lnT_dv)
            cx.dma(sy, dbg1, AB[:, :], "hw")
        if ffn_parts <= 1:
            fst.close()
            return

        rj_free = [0, 0]
        p16_rd = 0
        for e8 in range(8):
            for j in range(NJ):
                sidx = (e8 * NJ + j) % 4
                ssl = swt[sidx]
                ssl.wait_free(cx, g)
                dmark = cx.dma(g, ssl.ap[:, :], swb_in[e8, j, :, :], f"sws{sidx}")
                cx.wait(v, f"sws{sidx}", dmark)
                cx.wait(v, "dv", lnT_dv)
                cx.wait(v, "ac", rj_free[j % 2])
                dm2 = cx.inc(v.tensor_tensor(rj[j % 2][:, :], ssl.ap[:, :],
                                             AB[:, j * 1024: j * 1024 + 1024],
                                             ALU.mult), "dv")
                ssl.set_free("dv", dm2)
                cx.wait(s, "dv", dm2)
                if j == 0:
                    cx.wait(s, "dv", p16_rd)
                am = cx.inc(s.activation(ajunk[:, :], rj[j % 2][:, :], AF.Copy,
                                         accum_out=p16[:, j:j + 1]), "ac")
                rj_free[j % 2] = am
            cx.wait(v, "ac", rj_free[(NJ - 1) % 2])
            p16_rd = cx.inc(v.tensor_reduce(pacc[:, e8:e8 + 1], p16[:, :], AX.X,
                                            ALU.add), "dv")
        b = bank[2]
        b.wait_free(cx, t)
        cx.wait(t, "dv", cx.ctr["dv"])
        pm = cx.inc(t.matmul(b.ap[0:1, 0:8], ones_f[:, :], pacc[:, :],
                             start=True, stop=True), "pe")
        cx.wait(v, "pe", pm)
        dm = cx.inc(v.tensor_copy(r_row[0:1, 0:8], b.ap[0:1, 0:8]), "dv")
        b.set_free("dv", dm)
        cx.wait(sy, "dv", dm)
        hm = cx.dma(sy, r_ci[call].ap(), r_row[0:1, 0:8], "hr")
        cx.wait(g, "hr", hm)
        ccm = cx.inc(g.collective_compute("AllReduce", ALU.add, replica_groups=RG,
                                          ins=[r_ci[call].ap()], outs=[r_co[call].ap()]),
                     "cc")
        cx.wait(g, "cc", ccm)
        sm = cx.dma(g, r_row[0:1, 8:16], r_co[call].ap(), "swr")
        cx.wait(v, "swr", sm)
        LG = r_row[0:1, 8:16]
        def vsync(instr):
            mk = cx.inc(instr, "dv")
            cx.wait(v, "dv", mk)
            return mk
        vsync(v.tensor_tensor(LG, LG, swbias[0:1, :], ALU.add))
        M0, NM0 = r_row[0:1, 16:17], r_row[0:1, 17:18]
        vsync(v.reduce_max(M0, LG, axis=AX.X))
        dm = vsync(v.tensor_scalar(NM0, M0, -1.0, None, ALU.mult))
        EX = r_row[0:1, 24:32]
        cx.wait(s, "dv", dm)
        am = cx.inc(s.activation(EX, LG, AF.Exp, bias=NM0[0:1, 0:1]), "ac")
        SS, RC = r_row[0:1, 18:19], r_row[0:1, 19:20]
        cx.wait(v, "ac", am)
        vsync(v.reduce_sum(SS, EX, axis=AX.X))
        vsync(v.reciprocal(RC, SS))
        PRB = r_row[0:1, 32:40]
        vsync(v.tensor_single_scalar(PRB, EX, RC[0:1, 0:1], ALU.mult))
        G1, G2 = r_row[0:1, 20:21], r_row[0:1, 21:22]
        EQ1, EQ2, MS = r_row[0:1, 40:48], r_row[0:1, 48:56], r_row[0:1, 56:64]
        GV1, GV2 = r_row[0:1, 64:72], r_row[0:1, 72:80]
        vsync(v.reduce_max(G1, PRB, axis=AX.X))
        vsync(v.tensor_single_scalar(EQ1, PRB, G1[0:1, 0:1], ALU.is_equal))
        vsync(v.tensor_scalar(MS, EQ1, -4.0, None, ALU.mult))
        vsync(v.tensor_tensor(MS, MS, PRB, ALU.add))
        vsync(v.reduce_max(G2, MS, axis=AX.X))
        vsync(v.tensor_single_scalar(EQ2, MS, G2[0:1, 0:1], ALU.is_equal))
        vsync(v.tensor_single_scalar(GV1, EQ1, G1[0:1, 0:1], ALU.mult))
        dm = vsync(v.tensor_single_scalar(GV2, EQ2, G2[0:1, 0:1], ALU.mult))
        b = bank[3]
        b.wait_free(cx, t)
        cx.wait(t, "dv", dm)
        t.matmul(b.ap[:, 0:8], onesr_f[:, :], EQ1, start=True, stop=True)
        t.matmul(b.ap[:, 8:16], onesr_f[:, :], EQ2, start=True, stop=True)
        t.matmul(b.ap[:, 16:24], onesr_f[:, :], GV1, start=True, stop=True)
        pm = cx.inc(t.matmul(b.ap[:, 24:32], onesr_f[:, :], GV2, start=True, stop=True),
                    "pe")
        cx.wait(v, "pe", pm)
        dm = cx.inc(v.tensor_copy(selrep[:, :], b.ap[:, 0:32]), "dv")
        b.set_free("dv", dm)
        cx.wait(v, "sw12", sw0)
        vs_mk = [0]
        for k in range(2):
            for e8 in range(8):
                mcol = selrep[:, k * 8 + e8: k * 8 + e8 + 1]
                gcol = selrep[:, 16 + k * 8 + e8: 16 + k * 8 + e8 + 1]
                wsl = w1t_sb[:, e8 * P:(e8 + 1) * P]
                w2sl = w2t_sb[:, e8 * P:(e8 + 1) * P]
                if e8 == 0:
                    cx.inc(v.tensor_scalar(w1sel[k][:, :], wsl, mcol, None, ALU.mult), "dv")
                    cx.inc(v.tensor_scalar(w2gsel[k][:, :], w2sl, gcol, None, ALU.mult), "dv")
                else:
                    cx.inc(v.scalar_tensor_tensor(w1sel[k][:, :], wsl, mcol,
                                                  w1sel[k][:, :], ALU.mult, ALU.add), "dv")
                    cx.inc(v.scalar_tensor_tensor(w2gsel[k][:, :], w2sl, gcol,
                                                  w2gsel[k][:, :], ALU.mult, ALU.add), "dv")
            mk_ = cx.inc(v.tensor_tensor(tmp8[:, :], b1t[:, :], selrep[:, k * 8:(k + 1) * 8],
                                   ALU.mult), "dv")
            cx.wait(v, "dv", mk_)
            mk = cx.inc(v.tensor_reduce(b1sel[:, k:k + 1], tmp8[:, :], AX.X, ALU.add), "dv")
            cx.wait(v, "dv", mk)
        mk_ = cx.inc(v.tensor_tensor(tmp8[:, :], selrep[:, 16:24], selrep[:, 24:32], ALU.add), "dv")
        cx.wait(v, "dv", mk_)
        mk_ = cx.inc(v.tensor_tensor(tmp8[:, :], b2t[:, :], tmp8[:, :], ALU.mult), "dv")
        cx.wait(v, "dv", mk_)
        sel_dv = cx.inc(v.tensor_reduce(b2comb[:, :], tmp8[:, :], AX.X, ALU.add), "dv")
        cx.wait(v, "dv", sel_dv)
        sel_waited = [False]
        if ffn_parts <= 2:
            fst.close()
            return

        for half in range(2):
            if half == 1:
                cx.wait(v, "pe", cx.ctr["pe"])
                for ci in range(2):
                    ln_chunk(L, 2 + ci)
                    for j in range(NJ):
                        ln_apply(L, j, 2 + ci,
                                 AB[:, j * 1024 + ci * 512: j * 1024 + (ci + 1) * 512])
            ln_dv = cx.ctr["dv"]
            for ci in range(2):
                cc = half * 2 + ci
                for h in range(NH):
                    absl = AB[:, h * 1024 + ci * 512: h * 1024 + (ci + 1) * 512]
                    bo_ = bank[4 + h % 2]
                    am = 0
                    for k in range(2):
                        bh = bank[2 * k + h % 2]
                        bh.wait_free(cx, t)
                        cx.wait(t, "dv", max(ln_dv, sel_dv))
                        pm = cx.inc(t.matmul(bh.ap[:, :], w1sel[k][:, :], absl,
                                             start=True, stop=True), "pe")
                        cx.wait(s, "pe", pm)
                        if not sel_waited[0]:
                            cx.wait(s, "dv", sel_dv)
                            sel_waited[0] = True
                        if hsb_free[k][h % 2] is not None:
                            cx.wait(s, *hsb_free[k][h % 2])
                        am = cx.inc(s.activation(hsb[k][h % 2][:, :], bh.ap[:, :], AF.Gelu,
                                                 bias=b1sel[:, k:k + 1]), "ac")
                        bh.set_free("ac", am)
                    bo_.wait_free(cx, t)
                    cx.wait(t, "ac", am)
                    t.matmul(bo_.ap[:, :], w2gsel[0][:, :], hsb[0][h % 2][:, :],
                             start=True, stop=False)
                    pm = cx.inc(t.matmul(bo_.ap[:, :], w2gsel[1][:, :], hsb[1][h % 2][:, :],
                                         start=False, stop=True), "pe")
                    hsb_free[0][h % 2] = ("pe", pm)
                    hsb_free[1][h % 2] = ("pe", pm)
                    cx.wait(v, "pe", pm)
                    dm = cx.inc(v.scalar_tensor_tensor(absl, bo_.ap[:, :], b2comb[:, 0:1],
                                                       absl, ALU.add, ALU.add), "dv")
                    bo_.set_free("dv", dm)
            z_dv = cx.ctr["dv"]
            if dbg and call == 0 and half == 0:
                cx.wait(sy, "dv", z_dv)
                cx.dma(sy, dbg2, AB[:, :], "hw")
                cx.dma(sy, dbg3[0:1, :], r_row[0:1, :], "hw")
            if ffn_parts <= 3:
                continue
            for i in range(NJ):
                wsl = w3sb[i % 2]
                wsl.wait_free(cx, g)
                dmark = cx.dma(g, wsl.ap[:, :].rearrange("p (j b) -> p j b", b=P),
                               w3t_b[i, :, :, :], f"sw3{i % 2}")
                cx.wait(v, f"sw3{i % 2}", dmark)
                wdv = cx.inc(v.tensor_scalar(wsl.ap[:, :], wsl.ap[:, :], 0.5, None,
                                             ALU.mult), "dv")
                for ci in range(2):
                    bw = bank[6 + ci]
                    bw.wait_free(cx, t)
                    cx.wait(t, "dv", max(z_dv, wdv))
                    for j in range(NJ):
                        mm = t.matmul(bw.ap[:, :], wsl.ap[:, j * P:(j + 1) * P],
                                      AB[:, j * 1024 + ci * 512: j * 1024 + (ci + 1) * 512],
                                      start=(j == 0), stop=(j == NJ - 1))
                    pm = cx.inc(mm, "pe")
                    cc = half * 2 + ci
                    xsl = XT[:, i * TOWN + cc * 512: i * TOWN + (cc + 1) * 512]
                    cx.wait(v, "pe", pm)
                    dm = cx.inc(v.scalar_tensor_tensor(xsl, bw.ap[:, :], b3h[:, i:i + 1],
                                                       xsl, ALU.add, ALU.add), "dv")
                    bw.set_free("dv", dm)
                wsl.set_free("pe", pm)
        fst.close()

    def attention():
        qst = ExitStack()
        bvr = sb(qst, "c_bvr", [1, KV], BF16)
        cx.dma(g, bvr[:, :], bvr_in, "swv")
        L = make_ln_scratch(qst, "a")
        wv_sb = sb(qst, "wv_sb", [P, NJ * KV], BF16)
        wqk = [Slot(sb(qst, f"wqk{i}", [P, NJ * P], BF16), f"wqk{i}") for i in range(2)]
        ev16 = [Slot(sb(qst, "ev16_0", [P, 1024], BF16), "ev16_0")]
        wv_mark = cx.dma(g, wv_sb[:, :].rearrange("p (j k) -> p j k", k=KV),
                         wv_in, "swv")
        for half in range(2):
            hsl = slice(half * 1024, (half + 1) * 1024)
            if half == 1:
                cx.wait(v, "pe", cx.ctr["pe"])
            for ci in range(2):
                ln_chunk(L, half * 2 + ci)
                for j in range(NJ):
                    ln_apply(L, j, half * 2 + ci,
                             AB[:, j * 1024 + ci * 512: j * 1024 + (ci + 1) * 512])
            ln_dv = cx.ctr["dv"]
            for i in range(NH + NG):
                wsl = wqk[i % 2]
                wsl.wait_free(cx, g)
                src = wq_b[i, :, :, :] if i < NH else wk_b[i - NH, :, :, :]
                dmark = cx.dma(g, wsl.ap[:, :].rearrange("p (j b) -> p j b", b=P),
                               src, f"swq{i % 2}")
                b0, b1_ = bank[2 * (i % 2)], bank[2 * (i % 2) + 1]
                b0.wait_free(cx, t)
                b1_.wait_free(cx, t)
                cx.wait(t, f"swq{i % 2}", dmark)
                cx.wait(t, "dv", ln_dv)
                for ci, bw in ((0, b0), (1, b1_)):
                    for j in range(NJ):
                        mm = t.matmul(bw.ap[:, :], wsl.ap[:, j * P:(j + 1) * P],
                                      AB[:, j * 1024 + ci * 512: j * 1024 + (ci + 1) * 512],
                                      start=(j == 0), stop=(j == NJ - 1))
                pm = cx.inc(mm, "pe")
                wsl.set_free("pe", pm)
                esl = ev16[0]
                esl.wait_free(cx, v)
                cx.wait(v, "pe", pm)
                bias_col = bqt[:, i:i + 1] if i < NH else bkt[:, i - NH:i - NH + 1]
                v.tensor_scalar(esl.ap[:, 0:512], b0.ap[:, :], bias_col, None, ALU.add)
                dm = cx.inc(v.tensor_scalar(esl.ap[:, 512:1024], b1_.ap[:, :], bias_col,
                                            None, ALU.add), "dv")
                b0.set_free("dv", dm)
                b1_.set_free("dv", dm)
                cx.wait(sy, "dv", dm)
                dst = (qsc.ap()[i * P:(i + 1) * P, hsl] if i < NH else
                       kT_ci.ap()[(i - NH) * P:(i - NH + 1) * P, hsl])
                esl.set_free("hst", cx.dma(sy, dst, esl.ap[:, :], "hst"))
            for tbh in range(8):
                tb = half * 8 + tbh
                bv_ = bank[4 + tbh % 2]
                bv_.wait_free(cx, t)
                cx.wait(t, "swv", wv_mark)
                cx.wait(t, "dv", ln_dv)
                t.matmul(bv_.ap[:, :], onesr_bf[:, :], bvr[0:1, :], start=True, stop=False)
                for j in range(NJ):
                    lh = AB[:, j * 1024 + tbh * P: j * 1024 + (tbh + 1) * P]
                    mm = t.matmul(bv_.ap[:, :], lh, wv_sb[:, j * KV:(j + 1) * KV],
                                  start=False, stop=(j == NJ - 1))
                pm = cx.inc(mm, "pe")
                esl = ev16[0]
                esl.wait_free(cx, v)
                cx.wait(v, "pe", pm)
                dm = cx.inc(v.tensor_copy(esl.ap[:, 0:512], bv_.ap[:, :]), "dv")
                bv_.set_free("dv", dm)
                cx.wait(sy, "dv", dm)
                esl.set_free("hst", cx.dma(sy, v_ci.ap()[tb * P:(tb + 1) * P, :],
                                           esl.ap[:, 0:512], "hst"))
        qst.close()
        cx.barrier_all()

        cx.wait(g, "hst", cx.ctr["hst"])
        ck = cx.inc(g.collective_compute("AllGather", ALU.bypass, replica_groups=RG,
                                         ins=[kT_ci.ap()], outs=[kT_co.ap()]), "cck")
        cv = cx.inc(g.collective_compute("AllGather", ALU.bypass, replica_groups=RG,
                                         ins=[v_ci.ap()], outs=[v_co.ap()]), "ccv")

        if dbg:
            cx.wait(sy, "cck", ck)
            cx.wait(sy, "ccv", cv)
            cx.dma(sy, dbg1.rearrange("p (a t) -> p a t", t=TOWN),
                   kT_co.ap().rearrange("(a p) t -> p a t", p=P), "hw")
            cx.dma(sy, dbg2.rearrange("p (a t) -> p a t", t=KV),
                   v_co.ap().rearrange("(a p) t -> p a t", p=P), "hw")
        fst = ExitStack()
        MSK = sb(fst, "MSK", [P, 8 * 512], BF16)
        for z in range(8):
            for m in range(4):
                mi = v.tensor_single_scalar(MSK[:, z * 512 + m * P: z * 512 + (m + 1) * P],
                                            qmk[:, :], dtab[:, z * 4 + m: z * 4 + m + 1],
                                            ALU.is_ge)
                cx.inc(mi, "dv")
        msk_dv = cx.ctr["dv"]
        KT = Slot(sb(fst, "KTg", [P, 2 * TOWN], BF16), "KTg")
        VG = Slot(sb(fst, "VGg", [P, NKB * P], BF16), "VGg")
        QT = [Slot(sb(fst, f"QT{hh}", [P, 512], BF16), f"QT{hh}") for hh in range(2)]
        PT = [[Slot(sb(fst, f"PT{hh}_{pr}", [P, 512], BF16), f"PT{hh}_{pr}")
               for pr in range(2)] for hh in range(2)]
        rrec = sb(fst, "rrec", [P, 512], F32)
        wosb = [Slot(sb(fst, f"wosb{i}", [P, NJ * P], BF16), f"wosb{i}")
                for i in range(2)]

        def kcol(kb):
            return (kb % 2) * TOWN + (kb // 2) * P

        def vcol(kb):
            return ((kb % 2) * 16 + kb // 2) * P

        for half in range(2):
            for ci in range(2):
                cc = half * 2 + ci
                nkb = 8 * cc + 8
                nloc = nkb // 2
                for gi in range(NG):
                    KT.wait_free(cx, sy)
                    cx.wait(sy, "cck", ck)
                    cx.dma(sy, KT.ap[:, 0:nloc * P],
                           kT_co.ap()[gi * P:(gi + 1) * P, 0:nloc * P], "hkt")
                    kt_mark = cx.dma(sy, KT.ap[:, TOWN:TOWN + nloc * P],
                                     kT_co.ap()[KV + gi * P:KV + (gi + 1) * P, 0:nloc * P],
                                     "hkt")
                    VG.wait_free(cx, sy)
                    cx.wait(sy, "ccv", cv)
                    vg_mark = 0
                    for par2, vco0, vg0 in ((0, 0, 0), (1, TOWN, 16)):
                        for b0 in range(0, nloc, 4):
                            nb_ = min(4, nloc - b0)
                            vg_mark = cx.dma(
                                sy,
                                VG.ap[:, :].rearrange("p (k c) -> p k c", c=P)
                                [:, vg0 + b0:vg0 + b0 + nb_, :],
                                v_co.ap()[vco0 + b0 * P:vco0 + (b0 + nb_) * P,
                                          gi * P:(gi + 1) * P]
                                .rearrange("(k p) c -> p k c", p=P), "hvg")
                    for hp in range(2):
                        h0 = 4 * gi + 2 * hp
                        for hh in range(2):
                            QT[hh].wait_free(cx, sy)
                            QT[hh].load_mark = cx.dma(
                                sy, QT[hh].ap[:, :],
                                qsc.ap()[(h0 + hh) * P:(h0 + hh + 1) * P,
                                         cc * 512:(cc + 1) * 512], f"hq{hh}")
                        sbank = [[bank[0], bank[1]], [bank[2], bank[3]]]
                        cbank = [bank[4], bank[5]]
                        sumb = [bank[6], bank[7]]
                        for hh in range(2):
                            cbank[hh].wait_free(cx, t)
                            sumb[hh].wait_free(cx, t)
                        stage = []
                        def emit_scores(kb):
                            par = kb & 1
                            info = []
                            for hh in range(2):
                                bS = sbank[hh][par]
                                bS.wait_free(cx, t)
                                if kb < 2:
                                    cx.wait(t, "hkt", kt_mark)
                                    cx.wait(t, f"hq{hh}", QT[hh].load_mark)
                                pm = cx.inc(t.matmul(bS.ap[:, :],
                                                     KT.ap[:, kcol(kb):kcol(kb) + P],
                                                     QT[hh].ap[:, :],
                                                     start=True, stop=True), "pe")
                                info.append((bS, pm))
                            return info
                        def emit_exp_mask(kb, info):
                            par = kb & 1
                            zz = kb - 8 * cc
                            out = []
                            for hh in range(2):
                                bS, pm = info[hh]
                                cx.wait(s, "pe", pm)
                                am = cx.inc(s.activation(PT[hh][par].ap[:, :], bS.ap[:, :],
                                                         AF.Exp, bias=cshift[:, 0:1],
                                                         scale=SCALE), "ac")
                                bS.set_free("ac", am)
                                dvm = 0
                                if zz >= 0:
                                    cx.wait(v, "ac", am)
                                    cx.wait(v, "dv", msk_dv)
                                    dvm = cx.inc(
                                        v.tensor_tensor(PT[hh][par].ap[:, :],
                                                        PT[hh][par].ap[:, :],
                                                        MSK[:, zz * 512:(zz + 1) * 512],
                                                        ALU.mult), "dv")
                                out.append((am, dvm))
                            return out
                        def emit_consume(kb, marks):
                            par = kb & 1
                            pm = 0
                            for hh in range(2):
                                am, dvm = marks[hh]
                                if dvm:
                                    cx.wait(t, "dv", dvm)
                                else:
                                    cx.wait(t, "ac", am)
                                if kb == 0:
                                    cx.wait(t, "hvg", vg_mark)
                                t.matmul(sumb[hh].ap[:, :], ones_all[:, :],
                                         PT[hh][par].ap[:, :],
                                         start=(kb == 0), stop=(kb == nkb - 1))
                                pm = cx.inc(t.matmul(cbank[hh].ap[:, :],
                                                     VG.ap[:, vcol(kb):vcol(kb) + P],
                                                     PT[hh][par].ap[:, :],
                                                     start=(kb == 0), stop=(kb == nkb - 1)),
                                            "pe")
                                PT[hh][par].set_free("pe", pm)
                            return pm
                        prev = None
                        dump_unit = dbg and half == 0 and cc == 1 and gi == 3 and hp == 0
                        for kb in range(nkb):
                            sinfo = emit_scores(kb)
                            if prev is not None:
                                pm = emit_consume(prev[0], prev[1])
                            marks = emit_exp_mask(kb, sinfo)
                            if dump_unit:
                                for hx in range(2):
                                    amx, dvmx = marks[hx]
                                    cx.wait(sy, "ac", amx)
                                    if dvmx:
                                        cx.wait(sy, "dv", dvmx)
                                    dd = dbg1 if hx == 0 else dbg2
                                    hwm = cx.dma(sy, dd[:, kb * 512:(kb + 1) * 512],
                                                 PT[hx][kb & 1].ap[:, :], "hw")
                                cx.wait(s, "hw", hwm)
                            prev = (kb, marks)
                        pm = emit_consume(prev[0], prev[1])
                        QT[0].set_free("pe", pm)
                        QT[1].set_free("pe", pm)
                        last_pe = cx.ctr["pe"]
                        dm_prev = 0
                        for hh in range(2):
                            cx.wait(s, "pe", last_pe)
                            if dm_prev:
                                cx.wait(s, "dv", dm_prev)
                            lm = cx.inc(s.activation(rrec[:, :], sumb[hh].ap[:, :],
                                                     AF.Ln), "ac")
                            cx.wait(s, "ac", lm)
                            am = cx.inc(s.activation(rrec[:, :], rrec[:, :],
                                                     AF.Exp, scale=-1.0), "ac")
                            sumb[hh].set_free("ac", am)
                            cx.wait(v, "ac", am)
                            dst = AB[:, (h0 + hh) * 1024 + ci * 512:
                                     (h0 + hh) * 1024 + (ci + 1) * 512]
                            dm = cx.inc(v.tensor_tensor(dst, cbank[hh].ap[:, :],
                                                        rrec[:, :], ALU.mult), "dv")
                            cbank[hh].set_free("dv", dm)
                            dm_prev = dm
                            if dump_unit:
                                cx.wait(sy, "dv", dm)
                                hwm = cx.dma(sy, dbg2[:, 8192 + hh * 512: 8192 + (hh + 1) * 512],
                                             dst, "hw")
                                hwm = cx.dma(g, dbg2[:, 9216 + hh * 512: 9216 + (hh + 1) * 512],
                                             rrec[:, :], "sw")
                                cx.wait(v, "sw", hwm)
                    KT.set_free("pe", cx.ctr["pe"])
                    VG.set_free("pe", cx.ctr["pe"])
            ctx_dv = cx.ctr["dv"]
            if dbg and half == 0:
                cx.wait(sy, "dv", ctx_dv)
                cx.dma(sy, dbg1, AB[:, :], "hw")
            for i in range(NJ):
                wsl = wosb[i % 2]
                wsl.wait_free(cx, g)
                dmark = cx.dma(g, wsl.ap[:, :].rearrange("p (j b) -> p j b", b=P),
                               wo_b[i, :, :, :], f"swo{i % 2}")
                for ci in range(2):
                    bw = bank[6 + ci]
                    bw.wait_free(cx, t)
                    cx.wait(t, f"swo{i % 2}", dmark)
                    cx.wait(t, "dv", ctx_dv)
                    for j in range(NJ):
                        mm = t.matmul(bw.ap[:, :], wsl.ap[:, j * P:(j + 1) * P],
                                      AB[:, j * 1024 + ci * 512: j * 1024 + (ci + 1) * 512],
                                      start=(j == 0), stop=(j == NJ - 1))
                    pm = cx.inc(mm, "pe")
                    cc = half * 2 + ci
                    xsl = XT[:, i * TOWN + cc * 512: i * TOWN + (cc + 1) * 512]
                    cx.wait(v, "pe", pm)
                    dm = cx.inc(v.scalar_tensor_tensor(xsl, bw.ap[:, :], bot[:, i:i + 1],
                                                       xsl, ALU.add, ALU.add), "dv")
                    bw.set_free("dv", dm)
                wsl.set_free("pe", pm)
        fst.close()

    cx.barrier_all()
    if stages >= 1:
        ffn(0)
        cx.barrier_all()
    if stages >= 2:
        attention()
        cx.barrier_all()
    if stages >= 3:
        ffn(1)
        cx.barrier_all()

    with ExitStack() as iost:
        yout = [Slot(sb(iost, f"yout{i}", [P, D], F32), f"yout{i}") for i in range(2)]
        final_hw = 0
        for tb in range(NB):
            sl = yout[tb % 2]
            dm = 0
            for jg in range(4):
                b = bank[jg % 2]
                b.wait_free(cx, t)
                for jj in range(4):
                    j = jg * 4 + jj
                    src = XT[:, j * TOWN + tb * P: j * TOWN + (tb + 1) * P]
                    mm = t.transpose(b.ap[:, jj * P:(jj + 1) * P], src, ident[:, :])
                pm = cx.inc(mm, "pe")
                cx.wait(v, "pe", pm)
                sl.wait_free(cx, v)
                e = v.tensor_copy(sl.ap[:, jg * 512:(jg + 1) * 512], b.ap[:, :])
                dm = cx.inc(e, "dv")
                b.set_free("dv", dm)
            cx.wait(sy, "dv", dm)
            ysem = f"hy{tb % 2}"
            sl.set_free(ysem, cx.dma(sy, y_out[tb * P:(tb + 1) * P, :],
                                     sl.ap[:, :], ysem))
        for ysem in ("hy0", "hy1"):
            sy.wait_ge(cx.sem[ysem], cx.ctr[ysem])
            g.wait_ge(cx.sem[ysem], cx.ctr[ysem])

    stack.close()
    return nc


STAGES = 3
FFN_PARTS = 4
DBG = False
dbg = None


def kernel(L0_attn_hidden_states, L0_attn_Wq, L0_attn_bq, L0_attn_Wk, L0_attn_bk,
           L0_attn_Wv, L0_attn_bv, L0_attn_Wo, L0_attn_bo, L0_ffn_switch_w,
           L0_ffn_switch_b, L0_ffn_w1, L0_ffn_b1, L0_ffn_w2, L0_ffn_b2, L0_ffn_w3,
           L0_ffn_b3):
    x = np.ascontiguousarray(np.asarray(L0_attn_hidden_states, np.float32))
    B, S, _ = x.shape
    f32 = lambda a: np.ascontiguousarray(np.asarray(a, np.float32))

    Wq, bq = f32(L0_attn_Wq), f32(L0_attn_bq)
    Wk, bk = f32(L0_attn_Wk), f32(L0_attn_bk)
    Wv, bv = f32(L0_attn_Wv), f32(L0_attn_bv)
    Wo, bo = f32(L0_attn_Wo), f32(L0_attn_bo)
    sw, swb_ = f32(L0_ffn_switch_w), f32(L0_ffn_switch_b)
    w1, b1 = f32(L0_ffn_w1), f32(L0_ffn_b1)
    w2, b2 = f32(L0_ffn_w2), f32(L0_ffn_b2)
    w3, b3 = f32(L0_ffn_w3), f32(L0_ffn_b3)

    wq_b = np.ascontiguousarray(Wq.reshape(NJ, P, NH, P).transpose(2, 1, 0, 3))
    wk_b = np.ascontiguousarray(Wk.reshape(NJ, P, NG, P).transpose(2, 1, 0, 3))
    wo_b = np.ascontiguousarray(Wo.reshape(NJ, P, NJ, P).transpose(2, 1, 0, 3))
    w3t_b = np.ascontiguousarray(w3.reshape(NJ, P, NJ, P).transpose(0, 3, 2, 1))
    w1t = np.ascontiguousarray(w1.transpose(2, 0, 1))
    w2t = np.ascontiguousarray(w2.transpose(2, 0, 1))
    wv_b = np.ascontiguousarray(Wv.reshape(NJ, P, KV).transpose(1, 0, 2))
    b1t = np.ascontiguousarray(b1.T)
    b2t = np.ascontiguousarray(b2.T)
    bqt = np.ascontiguousarray(bq.reshape(NH, P).T)
    bkt = np.ascontiguousarray(bk.reshape(NG, P).T)
    bot = np.ascontiguousarray(bo.reshape(NJ, P).T)
    b3t = np.ascontiguousarray(b3.reshape(NJ, P).T)
    bvr = bv.reshape(1, KV)
    swbias = swb_.reshape(1, 8)
    sw5 = sw.reshape(8, 16, P, NJ, P)
    ident = np.eye(P, dtype=np.float32)
    qmk = (np.arange(P, dtype=np.float32)[None, :] -
           np.arange(P, dtype=np.float32)[:, None]).astype(np.float32)

    nc = bass.Bass("TRN2", target_bir_lowering=False, detect_race_conditions=False)
    build(nc, STAGES, FFN_PARTS, DBG)

    in_maps = []
    for c in range(8):
        b_, p_ = c // 2, c % 2
        xs = np.ascontiguousarray(x[b_].reshape(NKB, P, D)[p_::2].reshape(TOWN, D))
        dtabv = np.zeros((P, 32), np.float32)
        for z in range(8):
            for m in range(4):
                dtabv[:, z * 4 + m] = 128.0 * (z - 2 * m - p_)
        swb_c = np.ascontiguousarray(
            sw5[:, p_::2].transpose(0, 3, 4, 1, 2).reshape(8, NJ, P, 1024))
        in_maps.append({
            "x_in": xs, "qmk": qmk, "dtab": dtabv, "ident": ident,
            "wq_b": wq_b, "wk_b": wk_b, "wv": wv_b, "wo_b": wo_b, "w3t_b": w3t_b,
            "w1t": w1t, "w2t": w2t, "b1t": b1t, "b2t": b2t, "bqt": bqt, "bkt": bkt,
            "bvr": bvr, "bot": bot, "b3t": b3t, "swb": swb_c, "swbias": swbias,
        })
    res = run_bass_kernel_spmd(nc, in_maps, core_ids=list(range(8)))
    if DBG:
        global dbg
        dbg = [{k: np.asarray(res.results[c][k]) for k in ("dbg1", "dbg2", "dbg3")}
               for c in range(8)]
    out = np.empty((B, S, D), np.float32)
    for c in range(8):
        b_, p_ = c // 2, c % 2
        out[b_].reshape(NKB, P, D)[p_::2] = res.results[c]["y"].reshape(NB, P, D)
    return out
